# version 19
# baseline (speedup 1.0000x reference)
import math
from contextlib import ExitStack

import numpy as np
import ml_dtypes
import concourse.bass as bass
import concourse.mybir as mybir
from concourse.bass_utils import run_bass_kernel_spmd

F32 = mybir.dt.float32
BF16 = mybir.dt.bfloat16
AF = mybir.ActivationFunctionType
ALU = mybir.AluOpType
AX = mybir.AxisListType

S = 2048
D = 1024
H = 16
DLAT = 512
INTER = 4096
BLK = 512
NBLK = S // BLK
NT = BLK // 128
KT = 128
QSCALE = 1.0 / math.sqrt(96.0)
EPS = 1e-5

ENGINES = ("tensor", "vector", "scalar", "sync", "gpsimd")


class Build:

    def __init__(self, nc):
        self.nc = nc
        self.ops = {e: [] for e in ENGINES}
        self.cnt = {}
        self.handles = {}
        self.last_wait = {e: {} for e in ENGINES}

    def sem(self, name):
        if name not in self.handles:
            self.handles[name] = self.nc.alloc_semaphore(name)
            self.cnt[name] = 0
        return self.handles[name]

    def op(self, eng, emit, deps=(), inc=None, n=1):
        waits = []
        lw = self.last_wait[eng]
        for d in deps:
            if d is None:
                continue
            s, v = d
            if lw.get(s, 0) < v:
                waits.append((s, v))
                lw[s] = v
        fut = None
        incs = []
        if inc is not None:
            self.sem(inc)
            self.cnt[inc] += n
            incs = [(inc, n)]
            fut = (inc, self.cnt[inc])
        self.ops[eng].append((waits, emit, incs))
        return fut

    def play(self, eng_name, engine):
        for waits, emit, incs in self.ops[eng_name]:
            for s, v in waits:
                engine.wait_ge(self.handles[s], v)
            inst = emit(engine)
            for s, n in incs:
                inst.then_inc(self.handles[s], n)


class Arena:
    def __init__(self, tensor, cols):
        self.t = tensor
        self.cols = cols
        self.off = 0

    def alloc(self, n):
        assert self.off + n <= self.cols, f"arena overflow {self.off}+{n}>{self.cols}"
        ap = self.t[:, self.off:self.off + n]
        self.off += n
        return ap


def build_consts():
    ident = np.eye(128, dtype=ml_dtypes.bfloat16)
    c = np.arange(896)[None, :] - 384
    p = np.arange(128)[:, None]
    mask = (c >= p).astype(ml_dtypes.bfloat16)
    inv = 10000.0 ** (-np.arange(0, 32, 2, dtype=np.float64) / 32)
    pi = (np.arange(128) % 32) // 2
    ang = np.arange(S)[None, :] * inv[pi][:, None]
    cosT = np.cos(ang).astype(np.float32)
    sinT = np.sin(ang).astype(np.float32)
    sign = np.where(np.arange(128) % 2 == 0, -1.0, 1.0)[:, None]
    sinT = (sinT * sign).astype(np.float32)
    ones_row = np.ones((1, 128), dtype=ml_dtypes.bfloat16)
    return ident, mask, cosT, sinT, ones_row


def build_program(n_seqs=2, n_blocks=NBLK, debug=()):
    nc = bass.Bass("TRN2", target_bir_lowering=False, detect_race_conditions=False)
    bld = Build(nc)
    O = bld.op

    x_in = nc.dram_tensor("x", [2, S, D], F32, kind="ExternalInput").ap()
    w_in = {}
    for name, shape in [
        ("wdqt", [D, DLAT]), ("wdkvt", [D, DLAT]), ("wuqt", [DLAT, D]),
        ("wqrt", [DLAT, DLAT]), ("wqrst", [DLAT, DLAT]), ("wukt", [DLAT, D]),
        ("wuvt", [DLAT, D]), ("wkrt", [D, 128]), ("wkrst", [D, 128]),
        ("wot", [D, D]), ("w1", [D, INTER]), ("w2", [INTER, D]),
        ("b1l", [128, 32]), ("b2", [D]),
    ]:
        w_in[name] = nc.dram_tensor(name, shape, F32, kind="ExternalInput").ap()
    out = nc.dram_tensor("out", [2, S, D], F32, kind="ExternalOutput").ap()

    ident_np, mask_np, cos_np, sin_np, ones_np = build_consts()
    c_ident = nc.inline_tensor(ident_np, "c_ident").ap()
    c_mask = nc.inline_tensor(mask_np, "c_mask").ap()
    c_cos = nc.inline_tensor(cos_np, "c_cos").ap()
    c_sin = nc.inline_tensor(sin_np, "c_sin").ap()
    c_ones = nc.inline_tensor(ones_np, "c_ones").ap()

    wsc = {}
    for name in ("wdqt", "wdkvt", "wuqt", "wqrt", "wqrst", "wukt", "wuvt",
                 "wkrt", "wkrst", "wot", "w1", "w2"):
        wsc[name] = nc.dram_tensor("sc_" + name, list(w_in[name].shape), BF16,
                                   kind="Internal").ap()
    x1_sc = nc.dram_tensor("sc_x1", [2, S, D], F32, kind="Internal").ap()
    l2_sc = nc.dram_tensor("sc_l2t", [2 * NBLK, 8, 128, BLK], BF16,
                           kind="Internal").ap()

    dbg_out = {}
    for name, shape, dt in debug:
        dbg_out[name] = nc.dram_tensor("dbg_" + name, list(shape), dt,
                                       kind="ExternalOutput").ap()

    st = ExitStack()
    A16_COLS = 83400
    A32_COLS = 9800
    a16_t = st.enter_context(nc.sbuf_tensor("a16", [128, A16_COLS], BF16))
    a32_t = st.enter_context(nc.sbuf_tensor("a32", [128, A32_COLS], F32))
    a16 = Arena(a16_t[:], A16_COLS)
    a32 = Arena(a32_t[:], A32_COLS)

    sb_ident = a16.alloc(128)
    sb_mask = a16.alloc(896)
    sb_ones = a16.alloc(128)
    P_CONST_END = a16.off
    sb_kct = a16.alloc(8 * S)
    sb_krt = a16.alloc(S)
    sb_vaug = a16.alloc(16 * 1040)
    sb_wuqt = a16.alloc(4096)
    sb_warena = a16.alloc(8192)
    sb_ln1 = a16.alloc(2 * D)
    sb_ln1t = a16.alloc(8 * BLK)
    sb_cqt = a16.alloc(4 * BLK)
    sb_ckvt = a16.alloc(4 * BLK)
    sb_qct = a16.alloc(8 * BLK)
    sb_qr_raw = a16.alloc(2 * BLK)
    sb_qrs_raw = a16.alloc(2 * BLK)
    sb_qrope = a16.alloc(6 * BLK)
    sb_exp = a16.alloc(3 * 1024)
    sb_attn = a16.alloc(4 * D)
    sb_attnt = a16.alloc(8 * BLK)
    sb_ln2 = a16.alloc(2 * D)
    sb_ln2t = a16.alloc(2 * 1024)
    sb_x0 = a32.alloc(2 * D)
    sb_x0rr = a32.alloc(2 * D)
    sb_x1 = a32.alloc(2 * D)
    sb_cos = a32.alloc(BLK)
    sb_sin = a32.alloc(BLK)
    sb_scr = a32.alloc(D)
    sb_scr2 = a32.alloc(BLK)
    sb_rt1 = a32.alloc(BLK)
    sb_rt2 = a32.alloc(BLK)
    sb_stats = a32.alloc(16)
    sb_recip = a32.alloc(16)
    sb_b1l = a32.alloc(32)

    ps_a = st.enter_context(nc.psum_tensor("ps_a", [128, 512], F32))[:]
    ps_b = st.enter_context(nc.psum_tensor("ps_b", [128, 512], F32))[:]
    ps_sca = st.enter_context(nc.psum_tensor("ps_sca", [128, 1024], F32))[:]
    ps_scb = st.enter_context(nc.psum_tensor("ps_scb", [128, 1024], F32))[:]
    ps_pv0 = st.enter_context(nc.psum_tensor("ps_pv0", [128, 512], F32))[:]
    ps_pv1 = st.enter_context(nc.psum_tensor("ps_pv1", [128, 512], F32))[:]

    pc_last = None
    for name in wsc:
        src, dst = w_in[name], wsc[name]
        rows, cols = src.shape
        nchunk = rows // 128
        for c0 in range(nchunk):
            stage = sb_attnt[:, 0:cols].unsqueeze(1)
            f_ld = O("gpsimd", lambda g, s2=src, st2=stage, c2=c0: g.dma_start(
                out=st2, in_=s2.rearrange("(c p) n -> p c n", p=128)[:, c2:c2 + 1, :]),
                deps=[pc_last], inc="pc_l", n=16)
            pc_last = O("sync", lambda s_, d2=dst, st2=stage, c2=c0: s_.dma_start(
                out=d2.rearrange("(c p) n -> p c n", p=128)[:, c2:c2 + 1, :], in_=st2),
                deps=[f_ld], inc="pc_s", n=16)
    precast_done = pc_last
    pc_totals = [("pc_s", bld.cnt.get("pc_s", 0))]

    f_consts = O("sync", lambda s: s.dma_start(out=sb_ident, in_=c_ident), inc="s_ci", n=16)
    f_cmask = O("sync", lambda s: s.dma_start(out=sb_mask, in_=c_mask), inc="s_cm", n=16)
    f_cones = O("sync", lambda s: s.dma_start(out=sb_ones[0:1, :], in_=c_ones), inc="s_co", n=16)
    f_wuqt = O("gpsimd", lambda s: s.dma_start(
        out=sb_wuqt.rearrange("p (c n) -> p c n", n=D),
        in_=w_in["wuqt"].rearrange("(c p) n -> p c n", p=128)),
        inc="s_wuqt", n=16)

    def tap(name, ap, deps):
        if name in dbg_out:
            O("gpsimd", lambda g, a=ap, d=dbg_out[name]: g.dma_start(out=d, in_=a),
              deps=deps, inc="d_dbg", n=16)

    psum_proj = [ps_a, ps_b]
    proj_free = [None, None]
    rot = {"i": 0}

    def next_proj_psum():
        i = rot["i"]
        rot["i"] = (i + 1) % 2
        return i, psum_proj[i], proj_free[i]

    def layernorm_tile(xin_ap, out_ap, pref):
        stt = sb_stats
        f0 = O("vector", lambda v: v.reduce_sum(stt[:, 0:1], xin_ap, axis=AX.X),
               deps=pref, inc="dve")
        f1 = O("scalar", lambda sc: sc.memzero(stt[:, 1:2]), deps=pref, inc="act")
        f2 = O("scalar", lambda sc: sc.activation(sb_scr[:, 0:D], xin_ap, AF.Square,
                                                  accum_out=stt[:, 1:2]),
               deps=[f1], inc="act")
        f3 = O("vector", lambda v: v.tensor_scalar_mul(stt[:, 0:1], stt[:, 0:1], 1.0 / D),
               deps=[f0], inc="dve")
        f4 = O("vector", lambda v: v.tensor_scalar_mul(stt[:, 1:2], stt[:, 1:2], 1.0 / D),
               deps=[f2, f3], inc="dve")
        f5 = O("vector", lambda v: v.scalar_tensor_tensor(
            stt[:, 2:3], stt[:, 0:1], -1.0, stt[:, 0:1], ALU.mult, ALU.mult),
            deps=[f4], inc="dve")
        f6 = O("vector", lambda v: v.tensor_add(stt[:, 2:3], stt[:, 2:3], stt[:, 1:2]),
               deps=[f5], inc="dve")
        f7 = O("vector", lambda v: v.tensor_scalar_add(stt[:, 2:3], stt[:, 2:3], EPS),
               deps=[f6], inc="dve")
        f8 = O("scalar", lambda sc: sc.activation(stt[:, 3:4], stt[:, 2:3], AF.Ln),
               deps=[f7], inc="act")
        f9 = O("scalar", lambda sc: sc.activation(stt[:, 4:5], stt[:, 3:4], AF.Exp,
                                                  scale=np.float32(-0.5)),
               deps=[f8], inc="act")
        f10 = O("vector", lambda v: v.tensor_scalar(
            out_ap, xin_ap, stt[:, 0:1], stt[:, 4:5], ALU.subtract, ALU.mult),
            deps=[f9], inc="dve")
        return f10, f2

    def transpose128(src_ap, dst_ap, deps):
        i, ps, pfree = next_proj_psum()
        fm = O("tensor", lambda t, s2=src_ap, p=ps: t.matmul(
            p[:, 0:128], s2, sb_ident, start=True, stop=True),
            deps=list(deps) + [pfree, f_consts], inc="pe")
        fe = O("vector", lambda v, p=ps, d=dst_ap: v.tensor_copy(d, p[:, 0:128]),
               deps=[fm], inc="dve")
        proj_free[i] = fe
        return fe, fm

    def transpose_tile8(src_tile, dst_base_col, dst_buf, deps):
        evs = []
        last_fm = None
        for half in range(2):
            i, ps, pfree = next_proj_psum()
            fm = None
            for k in range(4):
                c = 4 * half + k
                fm = O("tensor", lambda t, c2=c, k2=k, p=ps, s2=src_tile: t.matmul(
                    p[:, 128 * k2:128 * k2 + 128],
                    s2[:, 128 * c2:128 * c2 + 128], sb_ident,
                    start=True, stop=True),
                    deps=(list(deps) + [pfree, f_consts]) if k == 0 else [],
                    inc="pe" if k == 3 else None)
            last_fm = fm
            dst = dst_buf[:, dst_base_col + 2048 * half:]
            dst3 = dst_buf.rearrange("p (c n) -> p c n", n=512)[
                :, 4 * half:4 * half + 4, dst_base_col % 512:dst_base_col % 512 + 128]
            fe = O("vector", lambda v, p=ps, d=dst3: v.tensor_copy(
                d, p[:, 0:512].rearrange("p (k n) -> p k n", n=128)),
                deps=[fm], inc="dve")
            proj_free[i] = fe
            evs.append(fe)
        return evs, last_fm

    def proj_chain(lhsT_slices, rhs_slices, psum_ap, deps, n=512, m=128):
        fm = None
        nk = len(lhsT_slices)
        for k in range(nk):
            fm = O("tensor", lambda t, l=lhsT_slices[k], r=rhs_slices[k],
                   p=psum_ap, st2=(k == 0), sp2=(k == nk - 1), n2=n, m2=m: t.matmul(
                       p[0:m2, 0:n2], l, r, start=st2, stop=sp2),
                   deps=(list(deps) if k == 0 else []),
                   inc="pe" if k == nk - 1 else None)
        return fm

    cs = {
        "x0_users": None,
        "tab_users": None,
        "arena_users": None,
        "psc_free": [None, None],
        "exp_free": [None, None, None],
        "pv_free": [None, None],
        "attn_seq_done": None,
        "x1_users": [None, None],
        "x0_slot": [None, None],
        "x0rr_slot": [None, None],
        "rope_scr": None,
        "ln1_users": [None, None],
        "ln2_users": [None, None],
        "l2t_slot": [None, None],
    }

    x1_store_cnt = [0]

    def build_seq(seq):
        f_vones = []
        for g in range(16):
            va = sb_vaug[:, 1040 * g:1040 * (g + 1)].rearrange(
                "p (h c) -> p h c", c=65)[:, :, 64:65]
            f_vones.append(O("vector", lambda v, a=va: v.memset(a, 1.0),
                             deps=[cs["attn_seq_done"]], inc="dve"))

        def early_proj_gen(j, pf):
            waves = {}

            def emit_waves():
                if waves:
                    return
                f_tab = O("sync", lambda s_, jj=j: s_.dma_start(
                    out=sb_cos, in_=c_cos[:, BLK * jj:BLK * (jj + 1)]),
                    deps=[cs["tab_users"]], inc="s_cos", n=16)
                f_tab2 = O("sync", lambda s_, jj=j: s_.dma_start(
                    out=sb_sin, in_=c_sin[:, BLK * jj:BLK * (jj + 1)]),
                    inc="s_sin", n=16)
                pf["f_tab"] = f_tab
                pf["f_tab2"] = f_tab2
                aph1 = [sb_warena[:, 0:4096], sb_warena[:, 4096:8192]]
                waves["ph1"] = O("gpsimd", lambda s_: s_.dma_start(
                    out=aph1[0].rearrange("p (c n) -> p c n", n=512),
                    in_=w_in["wdqt"].rearrange("(c p) n -> p c n", p=128)),
                    deps=[cs["arena_users"]], inc="s_dq", n=16)
                waves["ph1b"] = O("gpsimd", lambda s_: s_.dma_start(
                    out=aph1[1].rearrange("p (c n) -> p c n", n=512),
                    in_=w_in["wdkvt"].rearrange("(c p) n -> p c n", p=128)),
                    inc="s_dkv", n=16)
                waves["aph1"] = aph1

            ln1t_done = []
            flns = {}

            def emit_ln(g):
                sl = g % 2
                xt = sb_x0[:, D * sl:D * sl + D]
                f_x0 = O("sync", lambda s_, ss=seq, jj=j, g2=g, x2=xt: s_.dma_start(
                    out=x2,
                    in_=x_in[ss, BLK * jj + 128 * g2:BLK * jj + 128 * (g2 + 1), :]),
                    deps=cs["x0_slot"][sl] or [], inc="s_x0_%d" % sl, n=16)
                lt = sb_ln1[:, D * sl:D * sl + D]
                fln, fsq = layernorm_tile(xt, lt, [f_x0, cs["ln1_users"][sl]])
                cs["x0_slot"][sl] = [fln, fsq]
                flns[g] = (lt, fln)

            def emit_tr(g):
                sl = g % 2
                lt, fln = flns[g]
                evs, last_fm = transpose_tile8(lt, 128 * g, sb_ln1t, [fln])
                ln1t_done.extend(evs)
                cs["ln1_users"][sl] = last_fm

            emit_ln(0)
            yield
            emit_ln(1)
            yield
            emit_tr(0)
            yield
            emit_ln(2)
            yield
            emit_tr(1)
            yield
            emit_ln(3)
            yield
            emit_tr(2)
            yield
            emit_tr(3)
            emit_waves()
            yield
            pf["ln1t_done"] = ln1t_done
            f_ph1 = waves["ph1"]
            f_ph1b = waves["ph1b"]
            aph1 = waves["aph1"]

            rhs_l1 = [sb_ln1t[:, 512 * c:512 * (c + 1)] for c in range(8)]
            lat_done = []
            for m in range(4):
                for (wa, dst, fdep) in ((aph1[0], sb_cqt, f_ph1),
                                        (aph1[1], sb_ckvt, f_ph1b)):
                    i, ps, pfree = next_proj_psum()
                    fm = proj_chain(
                        [wa[:, 512 * c + 128 * m:512 * c + 128 * m + 128] for c in range(8)],
                        rhs_l1, ps, deps=ln1t_done + [fdep, pfree])
                    fe = O("vector", lambda v, p=ps, d=dst, mm=m: v.tensor_copy(
                        d[:, 512 * mm:512 * mm + 512], p[:, 0:512]),
                        deps=[fm], inc="dve")
                    proj_free[i] = fe
                    lat_done.append(fe)
                    yield
            pf["lat_done"] = lat_done

            akr = sb_warena[:, 4096:5120]
            akrs = sb_warena[:, 5120:6144]
            f_kr = O("gpsimd", lambda s_: s_.dma_start(
                out=akr.rearrange("p (c n) -> p c n", n=128),
                in_=w_in["wkrt"].rearrange("(c p) n -> p c n", p=128)),
                deps=lat_done, inc="s_kr", n=16)
            f_krs = O("gpsimd", lambda s_: s_.dma_start(
                out=akrs.rearrange("p (c n) -> p c n", n=128),
                in_=w_in["wkrst"].rearrange("(c p) n -> p c n", p=128)),
                inc="s_krs", n=16)
            i0, psk, pf0 = next_proj_psum()
            fmk = proj_chain([akr[:, 128 * c:128 * c + 128] for c in range(8)],
                             rhs_l1, psk, deps=ln1t_done + [f_kr, pf0])
            i1, psks, pf1 = next_proj_psum()
            fmks = proj_chain([akrs[:, 128 * c:128 * c + 128] for c in range(8)],
                              rhs_l1, psks, deps=[f_krs, pf1])
            fr1 = O("vector", lambda v, p=psk: v.tensor_tensor(
                sb_rt1, p[:, 0:512], sb_cos, ALU.mult),
                deps=[fmk, pf["f_tab"], cs["rope_scr"]], inc="dve")
            fr2 = O("vector", lambda v, p=psks: v.tensor_tensor(
                sb_rt2, p[:, 0:512], sb_sin, ALU.mult),
                deps=[fmks, pf["f_tab2"]], inc="dve")
            fkr = O("vector", lambda v, jj=j: v.tensor_add(
                sb_krt[:, BLK * jj:BLK * (jj + 1)], sb_rt1, sb_rt2),
                deps=[fr1, fr2], inc="dve")
            proj_free[i0] = fkr
            proj_free[i1] = fkr
            cs["rope_scr"] = fkr
            pf["fkr"] = fkr
            yield

            awuk = sb_warena[:, 0:4096]
            f_uk = O("gpsimd", lambda s_: s_.dma_start(
                out=awuk.rearrange("p (c n) -> p c n", n=D),
                in_=w_in["wukt"].rearrange("(c p) n -> p c n", p=128)),
                deps=lat_done, inc="s_uk", n=16)
            kct_done = []
            for m in range(8):
                i, ps, pfree = next_proj_psum()
                fm = proj_chain(
                    [awuk[:, 1024 * c + 128 * m:1024 * c + 128 * m + 128] for c in range(4)],
                    [sb_ckvt[:, 512 * c:512 * (c + 1)] for c in range(4)],
                    ps, deps=lat_done + [f_uk, pfree])
                fe = O("vector", lambda v, p=ps, mm=m, jj=j: v.tensor_copy(
                    sb_kct[:, S * mm + BLK * jj:S * mm + BLK * jj + 512], p[:, 0:512]),
                    deps=[fm], inc="dve")
                proj_free[i] = fe
                kct_done.append(fe)
                yield
            pf["kct_done"] = kct_done

            awuv = sb_warena[:, 4096:8192]
            f_uv = O("gpsimd", lambda s_: s_.dma_start(
                out=awuv.rearrange("p (c n) -> p c n", n=D),
                in_=w_in["wuvt"].rearrange("(c p) n -> p c n", p=128)),
                deps=[fkr], inc="s_uv", n=16)
            v_done = []
            for g in range(NT):
                for dh in range(2):
                    i, ps, pfree = next_proj_psum()
                    fm = proj_chain(
                        [sb_ckvt[:, 512 * c + 128 * g:512 * c + 128 * g + 128] for c in range(4)],
                        [awuv[:, 1024 * c + 512 * dh:1024 * c + 512 * dh + 512] for c in range(4)],
                        ps, deps=lat_done + [f_uv, pfree])
                    gg = NT * j + g
                    dst = sb_vaug[:, 1040 * gg:1040 * (gg + 1)].rearrange(
                        "p (h c) -> p h c", c=65)[:, 8 * dh:8 * dh + 8, 0:64]
                    fe = O("vector", lambda v, p=ps, d=dst: v.tensor_copy(
                        d, p[:, 0:512].rearrange("p (h c) -> p h c", c=64)),
                        deps=[fm, f_vones[gg]], inc="dve")
                    proj_free[i] = fe
                    v_done.append(fe)
                    yield
            pf["v_done"] = v_done

        def late_proj(j, pf):
            lat_done = pf["lat_done"]
            qct_done = []
            for m in range(8):
                i, ps, pfree = next_proj_psum()
                fm = proj_chain(
                    [sb_wuqt[:, 1024 * c + 128 * m:1024 * c + 128 * m + 128] for c in range(4)],
                    [sb_cqt[:, 512 * c:512 * (c + 1)] for c in range(4)],
                    ps, deps=lat_done + [f_wuqt, pfree])
                fe = O("vector", lambda v, p=ps, mm=m: v.tensor_copy(
                    sb_qct[:, 512 * mm:512 * mm + 512], p[:, 0:512]),
                    deps=[fm], inc="dve")
                proj_free[i] = fe
                qct_done.append(fe)
            pf["qct_done"] = qct_done

            aqr = sb_warena[:, 0:2048]
            aqrs = sb_warena[:, 2048:4096]
            f_qr = O("gpsimd", lambda s_: s_.dma_start(
                out=aqr.rearrange("p (c n) -> p c n", n=512),
                in_=w_in["wqrt"].rearrange("(c p) n -> p c n", p=128)),
                deps=pf["kct_done"], inc="s_qr", n=16)
            f_qrs = O("gpsimd", lambda s_: s_.dma_start(
                out=aqrs.rearrange("p (c n) -> p c n", n=512),
                in_=w_in["wqrst"].rearrange("(c p) n -> p c n", p=128)),
                inc="s_qrs", n=16)
            qrope_done = []
            for r in range(6):
                hlo = 96 * r
                hw = min(96, 512 - hlo)
                raw = sb_qr_raw[:, 512 * (r % 2):512 * (r % 2) + 512]
                raws = sb_qrs_raw[:, 512 * (r % 2):512 * (r % 2) + 512]
                i, ps, pfree = next_proj_psum()
                fm = proj_chain(
                    [aqr[:, 512 * c + hlo:512 * c + hlo + hw] for c in range(4)],
                    [sb_cqt[:, 512 * c:512 * (c + 1)] for c in range(4)],
                    ps, deps=lat_done + [f_qr, pfree], m=hw)
                fe = O("vector", lambda v, p=ps, d=raw, w2=hw: v.tensor_copy(
                    d[0:w2, :], p[0:w2, 0:512]), deps=[fm], inc="dve")
                proj_free[i] = fe
                i2, ps2_, pfree2 = next_proj_psum()
                fms = proj_chain(
                    [aqrs[:, 512 * c + hlo:512 * c + hlo + hw] for c in range(4)],
                    [sb_cqt[:, 512 * c:512 * (c + 1)] for c in range(4)],
                    ps2_, deps=[f_qrs, pfree2], m=hw)
                fes = O("vector", lambda v, p=ps2_, d=raws, w2=hw: v.tensor_copy(
                    d[0:w2, :], p[0:w2, 0:512]), deps=[fms], inc="dve")
                proj_free[i2] = fes
                ft1 = O("vector", lambda v, rw=raw, w2=hw: v.tensor_tensor(
                    sb_rt1[0:w2, :], rw[0:w2, :], sb_cos[0:w2, :], ALU.mult),
                    deps=[fe, pf["f_tab"], cs["rope_scr"]], inc="dve")
                ft2 = O("vector", lambda v, rs=raws, w2=hw: v.tensor_tensor(
                    sb_rt2[0:w2, :], rs[0:w2, :], sb_sin[0:w2, :], ALU.mult),
                    deps=[fes, pf["f_tab2"]], inc="dve")
                fq = O("vector", lambda v, rr=r, w2=hw: v.tensor_add(
                    sb_qrope[0:w2, 512 * rr:512 * rr + 512],
                    sb_rt1[0:w2, :], sb_rt2[0:w2, :]),
                    deps=[ft1, ft2], inc="dve")
                qrope_done.append(fq)
                cs["rope_scr"] = fq
            pf["qrope_done"] = qrope_done
            cs["tab_users"] = qrope_done[-1]

        def attention(j, pf, gen):
            nk = 4 * (j + 1)
            kct_done = pf["kct_done"]
            qct_done = pf["qct_done"]
            qrope_done = pf["qrope_done"]
            v_done = pf["v_done"]
            fkr = pf["fkr"]
            attn_evac_done = []
            for p_ in range(8):
                hA, hB = 2 * p_, 2 * p_ + 1

                def emit_scores(i_):
                    es = i_ % 3
                    psc = (ps_sca, ps_scb)[i_ % 2]
                    O("tensor", lambda t, pp=p_, ii=i_, ps2=psc: t.matmul(
                        ps2[:, 0:512],
                        sb_kct[0:64, S * pp + KT * ii:S * pp + KT * ii + 128],
                        sb_qct[0:64, 512 * pp:512 * pp + 512],
                        start=True, stop=False, tile_position=(0, 0)),
                        deps=kct_done + qct_done + [cs["psc_free"][i_ % 2]], inc=None)
                    O("tensor", lambda t, pp=p_, ii=i_, ps2=psc: t.matmul(
                        ps2[:, 512:1024],
                        sb_kct[64:128, S * pp + KT * ii:S * pp + KT * ii + 128],
                        sb_qct[64:128, 512 * pp:512 * pp + 512],
                        start=True, stop=False, tile_position=(64, 0)), inc=None)

                    def r_mm(h, col_off):
                        r = h // 3
                        b = 32 * (h % 3)
                        return O("tensor", lambda t, rr=r, bb=b, ii=i_, co=col_off,
                                 ps2=psc: t.matmul(
                            ps2[:, co:co + 512],
                            sb_krt[bb:bb + 32, KT * ii:KT * ii + 128],
                            sb_qrope[bb:bb + 32, 512 * rr:512 * rr + 512],
                            start=False, stop=True, tile_position=(bb, 0)),
                            deps=qrope_done + [fkr], inc="pe")
                    r_mm(hA, 0)
                    fr1_ = r_mm(hB, 512)
                    exp_ap = sb_exp[:, 1024 * es:1024 * es + 1024]
                    fexp = O("scalar", lambda sc, e=exp_ap, ps2=psc: sc.activation(
                        e, ps2, AF.Exp, scale=np.float32(QSCALE)),
                        deps=[fr1_, cs["exp_free"][es]], inc="act")
                    cs["psc_free"][i_ % 2] = fexp
                    fm_ = fexp
                    delta = KT * i_ - BLK * j
                    if delta >= 0:
                        mstart = 384 - delta
                        msl = sb_mask[:, mstart:mstart + 512]
                        fm_ = O("vector", lambda v, e=exp_ap, m2=msl: v.tensor_tensor(
                            e.rearrange("p (h q) -> p h q", q=512),
                            e.rearrange("p (h q) -> p h q", q=512),
                            m2.unsqueeze(1).broadcast_to([128, 2, 512]), ALU.mult),
                            deps=[fexp, f_cmask], inc="dve")
                    return fm_

                def emit_pv(i_, fm_):
                    es = i_ % 3
                    exp_ap = sb_exp[:, 1024 * es:1024 * es + 1024]
                    pv_last = None
                    for hh in range(2):
                        pvp = (ps_pv0, ps_pv1)[hh]
                        h = 2 * p_ + hh
                        for g in range(4):
                            pv_last = O("tensor", lambda t, e=exp_ap, hh2=hh, g2=g,
                                        h2=h, ii=i_, pv=pvp: t.matmul(
                                pv[:, 65 * g2:65 * g2 + 65],
                                e[:, 512 * hh2 + 128 * g2:512 * hh2 + 128 * g2 + 128],
                                sb_vaug[:, 1040 * ii + 65 * h2:1040 * ii + 65 * h2 + 65],
                                start=(ii == 0 and g2 == 0), stop=(ii == nk - 1)),
                                deps=[fm_] + v_done +
                                     ([cs["pv_free"][hh]] if i_ == 0 else []),
                                inc="pe" if (g == 3 and hh == 1) else None)
                    cs["exp_free"][es] = pv_last
                    return pv_last

                fms = {0: emit_scores(0)}
                if nk > 1:
                    fms[1] = emit_scores(1)
                pv_last = None
                for i_ in range(nk):
                    if i_ + 2 < nk:
                        fms[i_ + 2] = emit_scores(i_ + 2)
                    pv_last = emit_pv(i_, fms.pop(i_))
                    if gen is not None:
                        next(gen, None)
                        if j < 2:
                            next(gen, None)
                for hh in range(2):
                    h = 2 * p_ + hh
                    pvp = (ps_pv0, ps_pv1)[hh]
                    last = None
                    for g in range(4):
                        cix = 4 * hh + g
                        fr = O("vector", lambda v, pv=pvp, g2=g, c=cix: v.reciprocal(
                            sb_recip[:, c:c + 1], pv[:, 65 * g2 + 64:65 * g2 + 65]),
                            deps=[pv_last], inc="dve")
                        fe = O("vector", lambda v, pv=pvp, g2=g, h2=h, c=cix: v.tensor_scalar(
                            sb_attn[:, 1024 * g2 + 64 * h2:1024 * g2 + 64 * h2 + 64],
                            pv[:, 65 * g2:65 * g2 + 64],
                            sb_recip[:, c:c + 1], None, ALU.mult),
                            deps=[fr], inc="dve")
                        attn_evac_done.append(fe)
                        last = fe
                    cs["pv_free"][hh] = last
            if gen is not None:
                for _ in gen:
                    pass
            return attn_evac_done

        def finalize(j, pf_arena, attn_evac_done):
            attnt_done = []
            for g in range(NT):
                evs, _ = transpose_tile8(
                    sb_attn[:, 1024 * g:1024 * g + 1024], 128 * g, sb_attnt,
                    attn_evac_done + pc_totals)
                attnt_done.extend(evs)

            f_ph3 = O("gpsimd", lambda s_: s_.dma_start(
                out=sb_warena[:, 0:8192].rearrange("p (c n) -> p c n", n=1024),
                in_=w_in["wot"].rearrange("(c p) n -> p c n", p=128)),
                deps=pf_arena["qrope_done"] + pf_arena["v_done"], inc="s_wot", n=16)

            last_wo_mm = None
            blk_fin = []
            pend = {}
            for g in range(NT):
                sl = g % 2
                x0r = sb_x0rr[:, D * sl:D * sl + D]
                f_x0r = O("sync", lambda s_, ss=seq, jj=j, g2=g, x2=x0r: s_.dma_start(
                    out=x2,
                    in_=x_in[ss, BLK * jj + 128 * g2:BLK * jj + 128 * (g2 + 1), :]),
                    deps=cs["x0rr_slot"][sl] or [], inc="s_x0r_%d" % sl, n=16)
                x1t = sb_x1[:, D * sl:D * sl + D]
                halves = []
                for dh in range(2):
                    i, psx, pfree = next_proj_psum()
                    fm = proj_chain(
                        [sb_attnt[:, 512 * c + 128 * g:512 * c + 128 * g + 128] for c in range(8)],
                        [sb_warena[:, 1024 * c + 512 * dh:1024 * c + 512 * dh + 512] for c in range(8)],
                        psx, deps=attnt_done + [f_ph3, pfree])
                    last_wo_mm = fm
                    fadd = O("vector", lambda v, dh2=dh, xt=x1t, ps2=psx, x2=x0r: v.tensor_add(
                        xt[:, 512 * dh2:512 * dh2 + 512], ps2[:, 0:512],
                        x2[:, 512 * dh2:512 * dh2 + 512]),
                        deps=[fm, f_x0r] + ((cs["x1_users"][sl] or []) if dh == 0 else []),
                        inc="dve")
                    proj_free[i] = fadd
                    halves.append(fadd)
                cs["x0rr_slot"][sl] = halves
                f_st = O("sync", lambda s_, xt=x1t, ss=seq, jj=j, g2=g: s_.dma_start(
                    out=x1_sc[ss, BLK * jj + 128 * g2:BLK * jj + 128 * (g2 + 1), :],
                    in_=xt), deps=halves, inc="d_x1_%d" % sl, n=16)
                l2t = sb_ln2[:, D * sl:D * sl + D]
                fln2, _fsq2 = layernorm_tile(x1t, l2t, halves + [cs["ln2_users"][sl]])
                pend[g] = (sl, l2t, fln2)
                cs["x1_users"][sl] = [fln2, f_st]
                blk_fin.extend(halves + [f_st])

                def emit_tr2(g2x):
                    sl2, l2t2, fln2b = pend[g2x]
                    l2slot = sb_ln2t[:, 1024 * sl2:1024 * sl2 + 1024]
                    tr_evs = []
                    last_fm2 = None
                    for half in range(2):
                        i, ps, pfree = next_proj_psum()
                        fm2 = None
                        for k in range(4):
                            c = 4 * half + k
                            fm2 = O("tensor", lambda t, c2=c, k2=k, p=ps, s2=l2t2: t.matmul(
                                p[:, 128 * k2:128 * k2 + 128],
                                s2[:, 128 * c2:128 * c2 + 128], sb_ident,
                                start=True, stop=True),
                                deps=([fln2b, cs["l2t_slot"][sl2], pfree, f_consts]) if k == 0 else [],
                                inc="pe" if k == 3 else None)
                        last_fm2 = fm2
                        dst3 = l2slot.rearrange("p (c n) -> p c n", n=128)[
                            :, 4 * half:4 * half + 4, :]
                        fe = O("vector", lambda v, p=ps, d=dst3: v.tensor_copy(
                            d, p[:, 0:512].rearrange("p (k n) -> p k n", n=128)),
                            deps=[fm2], inc="dve")
                        proj_free[i] = fe
                        tr_evs.append(fe)
                    f_l2st = O("sync", lambda s_, ss=seq, jj=j, g3=g2x, slp=l2slot: s_.dma_start(
                        out=l2_sc[NBLK * ss + jj].rearrange("c p n -> p c n")[:, :, 128 * g3:128 * g3 + 128],
                        in_=slp.rearrange("p (c n) -> p c n", n=128)),
                        deps=tr_evs, inc="d_l2_%d" % sl2, n=16)
                    cs["l2t_slot"][sl2] = f_l2st
                    cs["ln2_users"][sl2] = last_fm2
                    blk_fin.extend(tr_evs + [f_l2st])

                if g >= 1:
                    emit_tr2(g - 1)
                if g == NT - 1:
                    emit_tr2(g)
            cs["arena_users"] = last_wo_mm
            return blk_fin

        pf = {}
        for _ in early_proj_gen(0, pf):
            pass
        late_proj(0, pf)
        for j in range(n_blocks):
            if j + 1 < n_blocks:
                pf_next = {}
                gen = early_proj_gen(j + 1, pf_next)
            else:
                pf_next, gen = None, None
            attn_ev = attention(j, pf, gen)
            if pf_next is not None:
                late_proj(j + 1, pf_next)
            finalize(j, pf_next if pf_next is not None else pf, attn_ev)
            if pf_next is not None:
                pf = pf_next
        cs["attn_seq_done"] = ("pe", bld.cnt.get("pe", 0))
        return


    for s_i in range(n_seqs):
        build_seq(s_i)

    fa16 = Arena(a16_t[:], A16_COLS)
    fa16.off = P_CONST_END
    f_w1r = fa16.alloc(4 * 1024)
    f_w2 = fa16.alloc(32 * D)
    f_ht = fa16.alloc(32 * BLK)
    f_l2t = fa16.alloc(2 * 8 * BLK)
    f_b2v = fa16.alloc(1024)
    fa32 = Arena(a32_t[:], A32_COLS)
    f_x1v = fa32.alloc(2 * D)
    f_x2v = fa32.alloc(2 * D)
    f_b1v = fa32.alloc(32)

    pbar = [("pe", bld.cnt.get("pe", 0)), ("dve", bld.cnt.get("dve", 0)),
            ("act", bld.cnt.get("act", 0))]
    st_totals = [(nm, bld.cnt[nm]) for nm in
                 ("d_x1_0", "d_x1_1", "d_l2_0", "d_l2_1") if nm in bld.cnt]

    fw2 = O("sync", lambda g: g.dma_start(
        out=f_w2.rearrange("p (c n) -> p c n", n=D),
        in_=wsc["w2"].rearrange("(c p) n -> p c n", p=128)),
        deps=pc_totals + pbar, inc="d_f2", n=16)
    fb1 = O("gpsimd", lambda g: g.dma_start(out=f_b1v, in_=w_in["b1l"]),
            deps=pbar, inc="d_f", n=16)
    fb2 = O("gpsimd", lambda g: g.dma_start(
        out=f_b2v[0:1, :], in_=w_in["b2"].unsqueeze(0)),
        deps=[fb1], inc="d_f", n=16)

    fps = {"i": 0, "free": [None, None]}

    def next_ffn_psum():
        i = fps["i"]
        fps["i"] = (i + 1) % 2
        return i, (ps_a, ps_b)[i], fps["free"][i]

    total_blocks = n_seqs * n_blocks
    w1_slot_free = [None, None, None, None]
    l2_free = [None, None]
    x1_slot_free = [None, None]
    x2_slot_free = [None, None]
    psx_free = [cs.get("psx_free")]
    for bi in range(total_blocks):
        s_, j_ = divmod(bi, NBLK)
        es = bi % 2
        l2v = f_l2t[:, 4096 * es:4096 * es + 4096]
        f_l2ld = O("sync", lambda sy, b2_=bi, l2=l2v: sy.dma_start(
            out=l2.rearrange("p (c n) -> p c n", n=512),
            in_=l2_sc[b2_].rearrange("c p n -> p c n")),
            deps=st_totals + [l2_free[es]] + pbar,
            inc="s_fl2_%d" % es, n=16)

        ht_done = []
        for ic in range(32):
            slot = f_w1r[:, 1024 * (ic % 4):1024 * (ic % 4) + 1024]
            f_w1ld = O("sync", lambda sy, sl=slot, ic2=ic: sy.dma_start(
                out=sl.rearrange("p (c n) -> p c n", n=128),
                in_=wsc["w1"].rearrange("(c p) n -> p c n", p=128)[:, :, 128 * ic2:128 * ic2 + 128]),
                deps=pc_totals + [w1_slot_free[ic % 4]] + pbar, inc="s_w1_%d" % (ic % 4), n=16)
            i, ps, pfree = next_ffn_psum()
            fm = proj_chain(
                [slot[:, 128 * c:128 * c + 128] for c in range(8)],
                [l2v[:, 512 * c:512 * c + 512] for c in range(8)],
                ps, deps=[f_l2ld, f_w1ld, pfree, fb1] + pbar)
            fe = O("scalar", lambda sc, p=ps, ic2=ic: sc.activation(
                f_ht[:, 512 * ic2:512 * ic2 + 512], p[:, 0:512], AF.Relu,
                bias=f_b1v[:, ic2:ic2 + 1]),
                deps=[fm], inc="act")
            fps["free"][i] = fe
            w1_slot_free[ic % 4] = fe
            ht_done.append(fe)
        l2_free[es] = ht_done[-1]

        for g in range(NT):
            x1v = f_x1v[:, D * (g % 2):D * (g % 2) + D]
            f_x1ld = O("sync", lambda sy, ss=s_, jj=j_, g2=g, xv=x1v: sy.dma_start(
                out=xv, in_=x1_sc[ss, BLK * jj + 128 * g2:BLK * jj + 128 * (g2 + 1), :]),
                deps=st_totals + [x1_slot_free[g % 2]] + pbar,
                inc="s_fx1_%d" % (g % 2), n=16)
            x2v = f_x2v[:, D * (g % 2):D * (g % 2) + D]
            halves = []
            for dh in range(2):
                i, psx, pfree = next_ffn_psum()
                O("tensor", lambda t, dh2=dh, ps2=psx: t.matmul(
                    ps2[:, 0:512], sb_ones[0:1, :],
                    f_b2v[0:1, 512 * dh2:512 * dh2 + 512],
                    start=True, stop=False),
                    deps=ht_done + [fb2, pfree, x2_slot_free[g % 2], f_cones], inc=None)
                fm = None
                for ic in range(32):
                    fm = O("tensor", lambda t, ic2=ic, g2=g, dh2=dh, ps2=psx: t.matmul(
                        ps2[:, 0:512],
                        f_ht[:, 512 * ic2 + 128 * g2:512 * ic2 + 128 * g2 + 128],
                        f_w2[:, 1024 * ic2 + 512 * dh2:1024 * ic2 + 512 * dh2 + 512],
                        start=False, stop=(ic2 == 31)),
                        deps=[fw2] if ic == 0 else [], inc="pe" if ic == 31 else None)
                fadd = O("vector", lambda v, xv=x2v, dh2=dh, xo=x1v, ps2=psx: v.tensor_add(
                    xv[:, 512 * dh2:512 * dh2 + 512], ps2[:, 0:512],
                    xo[:, 512 * dh2:512 * dh2 + 512]),
                    deps=[fm, f_x1ld], inc="dve")
                fps["free"][i] = fadd
                halves.append(fadd)
                psx_free[0] = fadd
            x1_slot_free[g % 2] = halves[-1]
            fst = O("sync", lambda sy, ss=s_, jj=j_, g2=g, xv=x2v: sy.dma_start(
                out=out[ss, BLK * jj + 128 * g2:BLK * jj + 128 * (g2 + 1), :], in_=xv),
                deps=halves, inc="d_out_%d" % (g % 2), n=16)
            x2_slot_free[g % 2] = fst

    O("gpsimd", lambda g: g.nop(),
      deps=[("d_out_0", bld.cnt.get("d_out_0", 0)),
            ("d_out_1", bld.cnt.get("d_out_1", 0)),
            (("d_dbg", bld.cnt["d_dbg"]) if "d_dbg" in bld.cnt else None)])

    with nc.Block() as block:
        @block.tensor
        def _(e):
            bld.play("tensor", e)

        @block.vector
        def _(e):
            bld.play("vector", e)

        @block.scalar
        def _(e):
            bld.play("scalar", e)

        @block.sync
        def _(e):
            bld.play("sync", e)

        @block.gpsimd
        def _(e):
            bld.play("gpsimd", e)

    st.close()
    return nc


def host_layouts(inputs):
    pair = np.arange(DLAT) ^ 1
    pr32 = np.arange(32) ^ 1
    w = {}
    w["wdqt"] = np.ascontiguousarray(inputs["L0_attn_W_DQ"].T)
    w["wdkvt"] = np.ascontiguousarray(inputs["L0_attn_W_DKV"].T)
    w["wuqt"] = np.ascontiguousarray(inputs["L0_attn_W_UQ"].T)
    wqrt = np.ascontiguousarray(inputs["L0_attn_W_QR"].T)
    w["wqrt"] = wqrt
    w["wqrst"] = np.ascontiguousarray(wqrt[:, pair])
    w["wukt"] = np.ascontiguousarray(inputs["L0_attn_W_UK"].T)
    w["wuvt"] = np.ascontiguousarray(inputs["L0_attn_W_UV"].T)
    wkrt = np.ascontiguousarray(inputs["L0_attn_W_KR"].T)
    w["wkrt"] = np.ascontiguousarray(np.tile(wkrt, (1, 4)))
    w["wkrst"] = np.ascontiguousarray(np.tile(wkrt[:, pr32], (1, 4)))
    w["wot"] = np.ascontiguousarray(inputs["L0_attn_W_O"].T)
    w["w1"] = inputs["L0_ffn_W1"]
    w["w2"] = inputs["L0_ffn_W2"]
    w["b1l"] = np.ascontiguousarray(inputs["L0_ffn_b1"].reshape(32, 128).T)
    w["b2"] = inputs["L0_ffn_b2"]
    return w


def kernel(**inputs):
    x = inputs["L0_attn_x"]
    w = host_layouts(inputs)
    nc = build_program()
    in_maps = []
    for c in range(8):
        m = {"x": np.ascontiguousarray(x[2 * c:2 * c + 2])}
        m.update(w)
        in_maps.append(m)
    res = run_bass_kernel_spmd(nc, in_maps, core_ids=list(range(8)))
    return np.concatenate([r["out"] for r in res.results], axis=0)


if __name__ == "__main__":
    print("building program...")
    nc = build_program()
    n = sum(1 for _ in nc.all_instructions())
    print("instructions:", n)


# version 20
# speedup vs baseline: 1.1632x; 1.1632x over previous
import math
from contextlib import ExitStack

import numpy as np
import ml_dtypes
import concourse.bass as bass
import concourse.mybir as mybir
from concourse.bass_utils import run_bass_kernel_spmd

F32 = mybir.dt.float32
BF16 = mybir.dt.bfloat16
AF = mybir.ActivationFunctionType
ALU = mybir.AluOpType
AX = mybir.AxisListType

S = 2048
D = 1024
H = 16
DLAT = 512
INTER = 4096
BLK = 512
NBLK = S // BLK
NT = BLK // 128
KT = 128
QSCALE = 1.0 / math.sqrt(96.0)
EPS = 1e-5

ENGINES = ("tensor", "vector", "scalar", "sync", "gpsimd")


class Build:

    def __init__(self, nc):
        self.nc = nc
        self.ops = {e: [] for e in ENGINES}
        self.cnt = {}
        self.handles = {}
        self.last_wait = {e: {} for e in ENGINES}

    def sem(self, name):
        if name not in self.handles:
            self.handles[name] = self.nc.alloc_semaphore(name)
            self.cnt[name] = 0
        return self.handles[name]

    def op(self, eng, emit, deps=(), inc=None, n=1):
        waits = []
        lw = self.last_wait[eng]
        for d in deps:
            if d is None:
                continue
            s, v = d
            if lw.get(s, 0) < v:
                waits.append((s, v))
                lw[s] = v
        fut = None
        incs = []
        if inc is not None:
            self.sem(inc)
            self.cnt[inc] += n
            incs = [(inc, n)]
            fut = (inc, self.cnt[inc])
        self.ops[eng].append((waits, emit, incs))
        return fut

    def play(self, eng_name, engine):
        for waits, emit, incs in self.ops[eng_name]:
            for s, v in waits:
                engine.wait_ge(self.handles[s], v)
            inst = emit(engine)
            for s, n in incs:
                inst.then_inc(self.handles[s], n)


class Arena:
    def __init__(self, tensor, cols):
        self.t = tensor
        self.cols = cols
        self.off = 0

    def alloc(self, n):
        assert self.off + n <= self.cols, f"arena overflow {self.off}+{n}>{self.cols}"
        ap = self.t[:, self.off:self.off + n]
        self.off += n
        return ap


def build_consts():
    ident = np.eye(128, dtype=ml_dtypes.bfloat16)
    c = np.arange(896)[None, :] - 384
    p = np.arange(128)[:, None]
    mask = (c >= p).astype(ml_dtypes.bfloat16)
    inv = 10000.0 ** (-np.arange(0, 32, 2, dtype=np.float64) / 32)
    pi = (np.arange(128) % 32) // 2
    ang = np.arange(S)[None, :] * inv[pi][:, None]
    cosT = np.cos(ang).astype(np.float32)
    sinT = np.sin(ang).astype(np.float32)
    sign = np.where(np.arange(128) % 2 == 0, -1.0, 1.0)[:, None]
    sinT = (sinT * sign).astype(np.float32)
    ones_row = np.ones((1, 128), dtype=ml_dtypes.bfloat16)
    return ident, mask, cosT, sinT, ones_row


def build_program(n_seqs=2, n_blocks=NBLK, debug=()):
    nc = bass.Bass("TRN2", target_bir_lowering=False, detect_race_conditions=False)
    bld = Build(nc)
    O = bld.op

    x_in = nc.dram_tensor("x", [2, S, D], F32, kind="ExternalInput").ap()
    w_in = {}
    for name, shape in [
        ("wdqt", [D, DLAT]), ("wdkvt", [D, DLAT]), ("wuqt", [DLAT, D]),
        ("wqrt", [DLAT, DLAT]), ("wqrst", [DLAT, DLAT]), ("wukt", [DLAT, D]),
        ("wuvt", [DLAT, D]), ("wkrt", [D, 128]), ("wkrst", [D, 128]),
        ("wot", [D, D]), ("w1", [D, INTER]), ("w2", [INTER, D]),
        ("b1l", [128, 32]), ("b2", [D]),
    ]:
        w_in[name] = nc.dram_tensor(name, shape, F32, kind="ExternalInput").ap()
    out = nc.dram_tensor("out", [2, S, D], F32, kind="ExternalOutput").ap()

    ident_np, mask_np, cos_np, sin_np, ones_np = build_consts()
    c_ident = nc.inline_tensor(ident_np, "c_ident").ap()
    c_mask = nc.inline_tensor(mask_np, "c_mask").ap()
    c_cos = nc.inline_tensor(cos_np, "c_cos").ap()
    c_sin = nc.inline_tensor(sin_np, "c_sin").ap()
    c_ones = nc.inline_tensor(ones_np, "c_ones").ap()

    wsc = {}
    for name in ("wdqt", "wdkvt", "wuqt", "wqrt", "wqrst", "wukt", "wuvt",
                 "wkrt", "wkrst", "wot", "w1", "w2"):
        wsc[name] = nc.dram_tensor("sc_" + name, list(w_in[name].shape), BF16,
                                   kind="Internal").ap()
    x1_sc = nc.dram_tensor("sc_x1", [2, S, D], F32, kind="Internal").ap()
    l2_sc = nc.dram_tensor("sc_l2t", [2 * NBLK, 8, 128, BLK], BF16,
                           kind="Internal").ap()

    dbg_out = {}
    for name, shape, dt in debug:
        dbg_out[name] = nc.dram_tensor("dbg_" + name, list(shape), dt,
                                       kind="ExternalOutput").ap()

    st = ExitStack()
    A16_COLS = 83400
    A32_COLS = 9800
    a16_t = st.enter_context(nc.sbuf_tensor("a16", [128, A16_COLS], BF16))
    a32_t = st.enter_context(nc.sbuf_tensor("a32", [128, A32_COLS], F32))
    a16 = Arena(a16_t[:], A16_COLS)
    a32 = Arena(a32_t[:], A32_COLS)

    sb_ident = a16.alloc(128)
    sb_mask = a16.alloc(896)
    sb_ones = a16.alloc(128)
    P_CONST_END = a16.off
    sb_kct = a16.alloc(8 * S)
    sb_krt = a16.alloc(S)
    sb_vaug = a16.alloc(16 * 1040)
    sb_wuqt = a16.alloc(4096)
    sb_warena = a16.alloc(8192)
    sb_ln1 = a16.alloc(2 * D)
    sb_ln1t = a16.alloc(8 * BLK)
    sb_cqt = a16.alloc(4 * BLK)
    sb_ckvt = a16.alloc(4 * BLK)
    sb_qct = a16.alloc(8 * BLK)
    sb_qr_raw = a16.alloc(2 * BLK)
    sb_qrs_raw = a16.alloc(2 * BLK)
    sb_qrope = a16.alloc(6 * BLK)
    sb_exp = a16.alloc(3 * 1024)
    sb_attn = a16.alloc(4 * D)
    sb_attnt = a16.alloc(8 * BLK)
    sb_ln2 = a16.alloc(2 * D)
    sb_ln2t = a16.alloc(2 * 1024)
    sb_x0 = a32.alloc(2 * D)
    sb_x0rr = a32.alloc(2 * D)
    sb_x1 = a32.alloc(2 * D)
    sb_cos = a32.alloc(BLK)
    sb_sin = a32.alloc(BLK)
    sb_scr = a32.alloc(D)
    sb_scr2 = a32.alloc(BLK)
    sb_rt1 = a32.alloc(BLK)
    sb_rt2 = a32.alloc(BLK)
    sb_stats = a32.alloc(16)
    sb_recip = a32.alloc(16)
    sb_b1l = a32.alloc(32)

    ps_a = st.enter_context(nc.psum_tensor("ps_a", [128, 512], F32))[:]
    ps_b = st.enter_context(nc.psum_tensor("ps_b", [128, 512], F32))[:]
    ps_sca = st.enter_context(nc.psum_tensor("ps_sca", [128, 1024], F32))[:]
    ps_scb = st.enter_context(nc.psum_tensor("ps_scb", [128, 1024], F32))[:]
    ps_pv0 = st.enter_context(nc.psum_tensor("ps_pv0", [128, 512], F32))[:]
    ps_pv1 = st.enter_context(nc.psum_tensor("ps_pv1", [128, 512], F32))[:]

    pc_slot_store = [None, None]
    pc_si = 0
    pc_last = None
    for name in wsc:
        src, dst = w_in[name], wsc[name]
        rows, cols = src.shape
        nchunk = rows // 128
        cpp = max(1, 4096 // cols)
        for c0 in range(0, nchunk, cpp):
            nch = min(cpp, nchunk - c0)
            si = pc_si
            pc_si = (pc_si + 1) % 2
            stage = sb_warena[:, 4096 * si:4096 * si + nch * cols].rearrange(
                "p (c n) -> p c n", n=cols)
            f_ld = O("gpsimd", lambda g, s2=src, st2=stage, c2=c0, n2=nch: g.dma_start(
                out=st2, in_=s2.rearrange("(c p) n -> p c n", p=128)[:, c2:c2 + n2, :]),
                deps=[pc_slot_store[si]], inc="pc_l_%d" % si, n=16)
            pc_last = O("sync", lambda s_, d2=dst, st2=stage, c2=c0, n2=nch: s_.dma_start(
                out=d2.rearrange("(c p) n -> p c n", p=128)[:, c2:c2 + n2, :], in_=st2),
                deps=[f_ld], inc="pc_s_%d" % si, n=16)
            pc_slot_store[si] = pc_last
    precast_done = pc_last
    pc_totals = [("pc_s_0", bld.cnt.get("pc_s_0", 0)), ("pc_s_1", bld.cnt.get("pc_s_1", 0))]

    f_consts = O("sync", lambda s: s.dma_start(out=sb_ident, in_=c_ident), inc="s_ci", n=16)
    f_cmask = O("sync", lambda s: s.dma_start(out=sb_mask, in_=c_mask), inc="s_cm", n=16)
    f_cones = O("sync", lambda s: s.dma_start(out=sb_ones[0:1, :], in_=c_ones), inc="s_co", n=16)
    f_wuqt = O("sync", lambda s: s.dma_start(
        out=sb_wuqt.rearrange("p (c n) -> p c n", n=D),
        in_=wsc["wuqt"].rearrange("(c p) n -> p c n", p=128)),
        deps=pc_totals, inc="s_wuqt", n=16)

    def tap(name, ap, deps):
        if name in dbg_out:
            O("gpsimd", lambda g, a=ap, d=dbg_out[name]: g.dma_start(out=d, in_=a),
              deps=deps, inc="d_dbg", n=16)

    psum_proj = [ps_a, ps_b]
    proj_free = [None, None]
    rot = {"i": 0}

    def next_proj_psum():
        i = rot["i"]
        rot["i"] = (i + 1) % 2
        return i, psum_proj[i], proj_free[i]

    def layernorm_tile(xin_ap, out_ap, pref):
        stt = sb_stats
        f0 = O("vector", lambda v: v.bn_stats(stt[:, 4:10], xin_ap[:, 0:512]),
               deps=pref, inc="dve")
        f1 = O("vector", lambda v: v.bn_stats(stt[:, 10:16], xin_ap[:, 512:1024]),
               deps=[f0], inc="dve")
        f2 = O("vector", lambda v: v.bn_aggr(stt[:, 0:2], stt[:, 4:16]),
               deps=[f1], inc="dve")
        f7 = O("vector", lambda v: v.tensor_scalar_add(stt[:, 2:3], stt[:, 1:2], EPS),
               deps=[f2], inc="dve")
        f8 = O("scalar", lambda sc: sc.activation(stt[:, 3:4], stt[:, 2:3], AF.Ln),
               deps=[f7], inc="act")
        f9 = O("scalar", lambda sc: sc.activation(stt[:, 4:5], stt[:, 3:4], AF.Exp,
                                                  scale=np.float32(-0.5)),
               deps=[f8], inc="act")
        f10 = O("vector", lambda v: v.tensor_scalar(
            out_ap, xin_ap, stt[:, 0:1], stt[:, 4:5], ALU.subtract, ALU.mult),
            deps=[f9], inc="dve")
        return f10, f10

    def transpose128(src_ap, dst_ap, deps):
        i, ps, pfree = next_proj_psum()
        fm = O("tensor", lambda t, s2=src_ap, p=ps: t.matmul(
            p[:, 0:128], s2, sb_ident, start=True, stop=True),
            deps=list(deps) + [pfree, f_consts], inc="pe")
        fe = O("vector", lambda v, p=ps, d=dst_ap: v.tensor_copy(d, p[:, 0:128]),
               deps=[fm], inc="dve")
        proj_free[i] = fe
        return fe, fm

    def transpose_tile8(src_tile, dst_base_col, dst_buf, deps):
        evs = []
        last_fm = None
        for half in range(2):
            i, ps, pfree = next_proj_psum()
            fm = None
            for k in range(4):
                c = 4 * half + k
                fm = O("tensor", lambda t, c2=c, k2=k, p=ps, s2=src_tile: t.matmul(
                    p[:, 128 * k2:128 * k2 + 128],
                    s2[:, 128 * c2:128 * c2 + 128], sb_ident,
                    start=True, stop=True),
                    deps=(list(deps) + [pfree, f_consts]) if k == 0 else [],
                    inc="pe" if k == 3 else None)
            last_fm = fm
            dst = dst_buf[:, dst_base_col + 2048 * half:]
            dst3 = dst_buf.rearrange("p (c n) -> p c n", n=512)[
                :, 4 * half:4 * half + 4, dst_base_col % 512:dst_base_col % 512 + 128]
            fe = O("vector", lambda v, p=ps, d=dst3: v.tensor_copy(
                d, p[:, 0:512].rearrange("p (k n) -> p k n", n=128)),
                deps=[fm], inc="dve")
            proj_free[i] = fe
            evs.append(fe)
        return evs, last_fm

    def proj_chain(lhsT_slices, rhs_slices, psum_ap, deps, n=512, m=128):
        fm = None
        nk = len(lhsT_slices)
        for k in range(nk):
            fm = O("tensor", lambda t, l=lhsT_slices[k], r=rhs_slices[k],
                   p=psum_ap, st2=(k == 0), sp2=(k == nk - 1), n2=n, m2=m: t.matmul(
                       p[0:m2, 0:n2], l, r, start=st2, stop=sp2),
                   deps=(list(deps) if k == 0 else []),
                   inc="pe" if k == nk - 1 else None)
        return fm

    cs = {
        "x0_users": None,
        "tab_users": None,
        "arena_users": None,
        "psc_free": [None, None],
        "exp_free": [None, None, None],
        "pv_free": [None, None],
        "attn_seq_done": None,
        "x1_users": [None, None],
        "x0_slot": [None, None],
        "x0rr_slot": [None, None],
        "rope_scr": None,
        "ln1_users": [None, None],
        "ln2_users": [None, None],
        "l2t_slot": [None, None],
    }

    x1_store_cnt = [0]

    def build_seq(seq):
        f_vones = []
        for g in range(16):
            va = sb_vaug[:, 1040 * g:1040 * (g + 1)].rearrange(
                "p (h c) -> p h c", c=65)[:, :, 64:65]
            f_vones.append(O("vector", lambda v, a=va: v.memset(a, 1.0),
                             deps=[cs["attn_seq_done"]], inc="dve"))

        def early_proj_gen(j, pf):
            waves = {}

            def emit_waves():
                if waves:
                    return
                f_tab = O("sync", lambda s_, jj=j: s_.dma_start(
                    out=sb_cos, in_=c_cos[:, BLK * jj:BLK * (jj + 1)]),
                    deps=[cs["tab_users"]], inc="s_cos", n=16)
                f_tab2 = O("sync", lambda s_, jj=j: s_.dma_start(
                    out=sb_sin, in_=c_sin[:, BLK * jj:BLK * (jj + 1)]),
                    inc="s_sin", n=16)
                pf["f_tab"] = f_tab
                pf["f_tab2"] = f_tab2
                aph1 = [sb_warena[:, 0:4096], sb_warena[:, 4096:8192]]
                waves["ph1"] = O("sync", lambda s_: s_.dma_start(
                    out=aph1[0].rearrange("p (c n) -> p c n", n=512),
                    in_=wsc["wdqt"].rearrange("(c p) n -> p c n", p=128)),
                    deps=pc_totals + [cs["arena_users"]], inc="s_dq", n=16)
                waves["ph1b"] = O("sync", lambda s_: s_.dma_start(
                    out=aph1[1].rearrange("p (c n) -> p c n", n=512),
                    in_=wsc["wdkvt"].rearrange("(c p) n -> p c n", p=128)),
                    inc="s_dkv", n=16)
                waves["aph1"] = aph1

            ln1t_done = []
            flns = {}

            def emit_ln(g):
                sl = g % 2
                xt = sb_x0[:, D * sl:D * sl + D]
                f_x0 = O("sync", lambda s_, ss=seq, jj=j, g2=g, x2=xt: s_.dma_start(
                    out=x2,
                    in_=x_in[ss, BLK * jj + 128 * g2:BLK * jj + 128 * (g2 + 1), :]),
                    deps=cs["x0_slot"][sl] or [], inc="s_x0_%d" % sl, n=16)
                lt = sb_ln1[:, D * sl:D * sl + D]
                fln, fsq = layernorm_tile(xt, lt, [f_x0, cs["ln1_users"][sl]])
                cs["x0_slot"][sl] = [fln, fsq]
                flns[g] = (lt, fln)

            def emit_tr(g):
                sl = g % 2
                lt, fln = flns[g]
                evs, last_fm = transpose_tile8(lt, 128 * g, sb_ln1t, [fln])
                ln1t_done.extend(evs)
                cs["ln1_users"][sl] = last_fm

            emit_ln(0)
            yield
            emit_ln(1)
            yield
            emit_tr(0)
            yield
            emit_ln(2)
            yield
            emit_tr(1)
            yield
            emit_ln(3)
            yield
            emit_tr(2)
            yield
            emit_tr(3)
            emit_waves()
            yield
            pf["ln1t_done"] = ln1t_done
            f_ph1 = waves["ph1"]
            f_ph1b = waves["ph1b"]
            aph1 = waves["aph1"]

            rhs_l1 = [sb_ln1t[:, 512 * c:512 * (c + 1)] for c in range(8)]
            lat_done = []
            for m in range(4):
                for (wa, dst, fdep) in ((aph1[0], sb_cqt, f_ph1),
                                        (aph1[1], sb_ckvt, f_ph1b)):
                    i, ps, pfree = next_proj_psum()
                    fm = proj_chain(
                        [wa[:, 512 * c + 128 * m:512 * c + 128 * m + 128] for c in range(8)],
                        rhs_l1, ps, deps=ln1t_done + [fdep, pfree])
                    fe = O("vector", lambda v, p=ps, d=dst, mm=m: v.tensor_copy(
                        d[:, 512 * mm:512 * mm + 512], p[:, 0:512]),
                        deps=[fm], inc="dve")
                    proj_free[i] = fe
                    lat_done.append(fe)
                    yield
            pf["lat_done"] = lat_done

            akr = sb_warena[:, 4096:5120]
            akrs = sb_warena[:, 5120:6144]
            f_kr = O("sync", lambda s_: s_.dma_start(
                out=akr.rearrange("p (c n) -> p c n", n=128),
                in_=wsc["wkrt"].rearrange("(c p) n -> p c n", p=128)),
                deps=lat_done, inc="s_kr", n=16)
            f_krs = O("sync", lambda s_: s_.dma_start(
                out=akrs.rearrange("p (c n) -> p c n", n=128),
                in_=wsc["wkrst"].rearrange("(c p) n -> p c n", p=128)),
                inc="s_krs", n=16)
            i0, psk, pf0 = next_proj_psum()
            fmk = proj_chain([akr[:, 128 * c:128 * c + 128] for c in range(8)],
                             rhs_l1, psk, deps=ln1t_done + [f_kr, pf0])
            i1, psks, pf1 = next_proj_psum()
            fmks = proj_chain([akrs[:, 128 * c:128 * c + 128] for c in range(8)],
                              rhs_l1, psks, deps=[f_krs, pf1])
            fr1 = O("vector", lambda v, p=psk: v.tensor_tensor(
                sb_rt1, p[:, 0:512], sb_cos, ALU.mult),
                deps=[fmk, pf["f_tab"], cs["rope_scr"]], inc="dve")
            fr2 = O("vector", lambda v, p=psks: v.tensor_tensor(
                sb_rt2, p[:, 0:512], sb_sin, ALU.mult),
                deps=[fmks, pf["f_tab2"]], inc="dve")
            fkr = O("vector", lambda v, jj=j: v.tensor_add(
                sb_krt[:, BLK * jj:BLK * (jj + 1)], sb_rt1, sb_rt2),
                deps=[fr1, fr2], inc="dve")
            proj_free[i0] = fkr
            proj_free[i1] = fkr
            cs["rope_scr"] = fkr
            pf["fkr"] = fkr
            yield

            awuk = sb_warena[:, 0:4096]
            f_uk = O("sync", lambda s_: s_.dma_start(
                out=awuk.rearrange("p (c n) -> p c n", n=D),
                in_=wsc["wukt"].rearrange("(c p) n -> p c n", p=128)),
                deps=lat_done, inc="s_uk", n=16)
            kct_done = []
            for m in range(8):
                i, ps, pfree = next_proj_psum()
                fm = proj_chain(
                    [awuk[:, 1024 * c + 128 * m:1024 * c + 128 * m + 128] for c in range(4)],
                    [sb_ckvt[:, 512 * c:512 * (c + 1)] for c in range(4)],
                    ps, deps=lat_done + [f_uk, pfree])
                fe = O("vector", lambda v, p=ps, mm=m, jj=j: v.tensor_copy(
                    sb_kct[:, S * mm + BLK * jj:S * mm + BLK * jj + 512], p[:, 0:512]),
                    deps=[fm], inc="dve")
                proj_free[i] = fe
                kct_done.append(fe)
                yield
            pf["kct_done"] = kct_done

            awuv = sb_warena[:, 4096:8192]
            f_uv = O("sync", lambda s_: s_.dma_start(
                out=awuv.rearrange("p (c n) -> p c n", n=D),
                in_=wsc["wuvt"].rearrange("(c p) n -> p c n", p=128)),
                deps=[fkr], inc="s_uv", n=16)
            v_done = []
            for g in range(NT):
                for dh in range(2):
                    i, ps, pfree = next_proj_psum()
                    fm = proj_chain(
                        [sb_ckvt[:, 512 * c + 128 * g:512 * c + 128 * g + 128] for c in range(4)],
                        [awuv[:, 1024 * c + 512 * dh:1024 * c + 512 * dh + 512] for c in range(4)],
                        ps, deps=lat_done + [f_uv, pfree])
                    gg = NT * j + g
                    dst = sb_vaug[:, 1040 * gg:1040 * (gg + 1)].rearrange(
                        "p (h c) -> p h c", c=65)[:, 8 * dh:8 * dh + 8, 0:64]
                    fe = O("vector", lambda v, p=ps, d=dst: v.tensor_copy(
                        d, p[:, 0:512].rearrange("p (h c) -> p h c", c=64)),
                        deps=[fm, f_vones[gg]], inc="dve")
                    proj_free[i] = fe
                    v_done.append(fe)
                    yield
            pf["v_done"] = v_done

        def late_proj(j, pf):
            lat_done = pf["lat_done"]
            qct_done = []
            for m in range(8):
                i, ps, pfree = next_proj_psum()
                fm = proj_chain(
                    [sb_wuqt[:, 1024 * c + 128 * m:1024 * c + 128 * m + 128] for c in range(4)],
                    [sb_cqt[:, 512 * c:512 * (c + 1)] for c in range(4)],
                    ps, deps=lat_done + [f_wuqt, pfree])
                fe = O("vector", lambda v, p=ps, mm=m: v.tensor_copy(
                    sb_qct[:, 512 * mm:512 * mm + 512], p[:, 0:512]),
                    deps=[fm], inc="dve")
                proj_free[i] = fe
                qct_done.append(fe)
            pf["qct_done"] = qct_done

            aqr = sb_warena[:, 0:2048]
            aqrs = sb_warena[:, 2048:4096]
            f_qr = O("sync", lambda s_: s_.dma_start(
                out=aqr.rearrange("p (c n) -> p c n", n=512),
                in_=wsc["wqrt"].rearrange("(c p) n -> p c n", p=128)),
                deps=pf["kct_done"], inc="s_qr", n=16)
            f_qrs = O("sync", lambda s_: s_.dma_start(
                out=aqrs.rearrange("p (c n) -> p c n", n=512),
                in_=wsc["wqrst"].rearrange("(c p) n -> p c n", p=128)),
                inc="s_qrs", n=16)
            qrope_done = []
            for r in range(6):
                hlo = 96 * r
                hw = min(96, 512 - hlo)
                raw = sb_qr_raw[:, 512 * (r % 2):512 * (r % 2) + 512]
                raws = sb_qrs_raw[:, 512 * (r % 2):512 * (r % 2) + 512]
                i, ps, pfree = next_proj_psum()
                fm = proj_chain(
                    [aqr[:, 512 * c + hlo:512 * c + hlo + hw] for c in range(4)],
                    [sb_cqt[:, 512 * c:512 * (c + 1)] for c in range(4)],
                    ps, deps=lat_done + [f_qr, pfree], m=hw)
                fe = O("vector", lambda v, p=ps, d=raw, w2=hw: v.tensor_copy(
                    d[0:w2, :], p[0:w2, 0:512]), deps=[fm], inc="dve")
                proj_free[i] = fe
                i2, ps2_, pfree2 = next_proj_psum()
                fms = proj_chain(
                    [aqrs[:, 512 * c + hlo:512 * c + hlo + hw] for c in range(4)],
                    [sb_cqt[:, 512 * c:512 * (c + 1)] for c in range(4)],
                    ps2_, deps=[f_qrs, pfree2], m=hw)
                fes = O("vector", lambda v, p=ps2_, d=raws, w2=hw: v.tensor_copy(
                    d[0:w2, :], p[0:w2, 0:512]), deps=[fms], inc="dve")
                proj_free[i2] = fes
                ft1 = O("vector", lambda v, rw=raw, w2=hw: v.tensor_tensor(
                    sb_rt1[0:w2, :], rw[0:w2, :], sb_cos[0:w2, :], ALU.mult),
                    deps=[fe, pf["f_tab"], cs["rope_scr"]], inc="dve")
                ft2 = O("vector", lambda v, rs=raws, w2=hw: v.tensor_tensor(
                    sb_rt2[0:w2, :], rs[0:w2, :], sb_sin[0:w2, :], ALU.mult),
                    deps=[fes, pf["f_tab2"]], inc="dve")
                fq = O("vector", lambda v, rr=r, w2=hw: v.tensor_add(
                    sb_qrope[0:w2, 512 * rr:512 * rr + 512],
                    sb_rt1[0:w2, :], sb_rt2[0:w2, :]),
                    deps=[ft1, ft2], inc="dve")
                qrope_done.append(fq)
                cs["rope_scr"] = fq
            pf["qrope_done"] = qrope_done
            cs["tab_users"] = qrope_done[-1]

        def attention(j, pf, gen):
            nk = 4 * (j + 1)
            kct_done = pf["kct_done"]
            qct_done = pf["qct_done"]
            qrope_done = pf["qrope_done"]
            v_done = pf["v_done"]
            fkr = pf["fkr"]
            attn_evac_done = []
            for p_ in range(8):
                hA, hB = 2 * p_, 2 * p_ + 1

                def emit_scores(i_):
                    es = i_ % 3
                    psc = (ps_sca, ps_scb)[i_ % 2]
                    O("tensor", lambda t, pp=p_, ii=i_, ps2=psc: t.matmul(
                        ps2[:, 0:512],
                        sb_kct[0:64, S * pp + KT * ii:S * pp + KT * ii + 128],
                        sb_qct[0:64, 512 * pp:512 * pp + 512],
                        start=True, stop=False, tile_position=(0, 0)),
                        deps=kct_done + qct_done + [cs["psc_free"][i_ % 2]], inc=None)
                    O("tensor", lambda t, pp=p_, ii=i_, ps2=psc: t.matmul(
                        ps2[:, 512:1024],
                        sb_kct[64:128, S * pp + KT * ii:S * pp + KT * ii + 128],
                        sb_qct[64:128, 512 * pp:512 * pp + 512],
                        start=True, stop=False, tile_position=(64, 0)), inc=None)

                    def r_mm(h, col_off):
                        r = h // 3
                        b = 32 * (h % 3)
                        return O("tensor", lambda t, rr=r, bb=b, ii=i_, co=col_off,
                                 ps2=psc: t.matmul(
                            ps2[:, co:co + 512],
                            sb_krt[bb:bb + 32, KT * ii:KT * ii + 128],
                            sb_qrope[bb:bb + 32, 512 * rr:512 * rr + 512],
                            start=False, stop=True, tile_position=(bb, 0)),
                            deps=qrope_done + [fkr], inc="pe")
                    r_mm(hA, 0)
                    fr1_ = r_mm(hB, 512)
                    exp_ap = sb_exp[:, 1024 * es:1024 * es + 1024]
                    fexp = O("scalar", lambda sc, e=exp_ap, ps2=psc: sc.activation(
                        e, ps2, AF.Exp, scale=np.float32(QSCALE)),
                        deps=[fr1_, cs["exp_free"][es]], inc="act")
                    cs["psc_free"][i_ % 2] = fexp
                    fm_ = fexp
                    delta = KT * i_ - BLK * j
                    if delta >= 0:
                        mstart = 384 - delta
                        msl = sb_mask[:, mstart:mstart + 512]
                        fm_ = O("vector", lambda v, e=exp_ap, m2=msl: v.tensor_tensor(
                            e.rearrange("p (h q) -> p h q", q=512),
                            e.rearrange("p (h q) -> p h q", q=512),
                            m2.unsqueeze(1).broadcast_to([128, 2, 512]), ALU.mult),
                            deps=[fexp, f_cmask], inc="dve")
                    return fm_

                def emit_pv(i_, fm_):
                    es = i_ % 3
                    exp_ap = sb_exp[:, 1024 * es:1024 * es + 1024]
                    pv_last = None
                    for hh in range(2):
                        pvp = (ps_pv0, ps_pv1)[hh]
                        h = 2 * p_ + hh
                        for g in range(4):
                            pv_last = O("tensor", lambda t, e=exp_ap, hh2=hh, g2=g,
                                        h2=h, ii=i_, pv=pvp: t.matmul(
                                pv[:, 65 * g2:65 * g2 + 65],
                                e[:, 512 * hh2 + 128 * g2:512 * hh2 + 128 * g2 + 128],
                                sb_vaug[:, 1040 * ii + 65 * h2:1040 * ii + 65 * h2 + 65],
                                start=(ii == 0 and g2 == 0), stop=(ii == nk - 1)),
                                deps=[fm_] + v_done +
                                     ([cs["pv_free"][hh]] if i_ == 0 else []),
                                inc="pe" if (g == 3 and hh == 1) else None)
                    cs["exp_free"][es] = pv_last
                    return pv_last

                fms = {0: emit_scores(0)}
                if nk > 1:
                    fms[1] = emit_scores(1)
                pv_last = None
                for i_ in range(nk):
                    if i_ + 2 < nk:
                        fms[i_ + 2] = emit_scores(i_ + 2)
                    pv_last = emit_pv(i_, fms.pop(i_))
                    if gen is not None:
                        next(gen, None)
                        if j < 2:
                            next(gen, None)
                for hh in range(2):
                    h = 2 * p_ + hh
                    pvp = (ps_pv0, ps_pv1)[hh]
                    last = None
                    for g in range(4):
                        cix = 4 * hh + g
                        fr = O("vector", lambda v, pv=pvp, g2=g, c=cix: v.reciprocal(
                            sb_recip[:, c:c + 1], pv[:, 65 * g2 + 64:65 * g2 + 65]),
                            deps=[pv_last], inc="dve")
                        fe = O("vector", lambda v, pv=pvp, g2=g, h2=h, c=cix: v.tensor_scalar(
                            sb_attn[:, 1024 * g2 + 64 * h2:1024 * g2 + 64 * h2 + 64],
                            pv[:, 65 * g2:65 * g2 + 64],
                            sb_recip[:, c:c + 1], None, ALU.mult),
                            deps=[fr], inc="dve")
                        attn_evac_done.append(fe)
                        last = fe
                    cs["pv_free"][hh] = last
            if gen is not None:
                for _ in gen:
                    pass
            return attn_evac_done

        def finalize(j, pf_arena, attn_evac_done):
            attnt_done = []
            for g in range(NT):
                evs, _ = transpose_tile8(
                    sb_attn[:, 1024 * g:1024 * g + 1024], 128 * g, sb_attnt,
                    attn_evac_done)
                attnt_done.extend(evs)

            f_ph3 = O("sync", lambda s_: s_.dma_start(
                out=sb_warena[:, 0:8192].rearrange("p (c n) -> p c n", n=1024),
                in_=wsc["wot"].rearrange("(c p) n -> p c n", p=128)),
                deps=pf_arena["qrope_done"] + pf_arena["v_done"], inc="s_wot", n=16)

            last_wo_mm = None
            blk_fin = []
            pend = {}
            for g in range(NT):
                sl = g % 2
                x0r = sb_x0rr[:, D * sl:D * sl + D]
                f_x0r = O("sync", lambda s_, ss=seq, jj=j, g2=g, x2=x0r: s_.dma_start(
                    out=x2,
                    in_=x_in[ss, BLK * jj + 128 * g2:BLK * jj + 128 * (g2 + 1), :]),
                    deps=cs["x0rr_slot"][sl] or [], inc="s_x0r_%d" % sl, n=16)
                x1t = sb_x1[:, D * sl:D * sl + D]
                halves = []
                for dh in range(2):
                    i, psx, pfree = next_proj_psum()
                    fm = proj_chain(
                        [sb_attnt[:, 512 * c + 128 * g:512 * c + 128 * g + 128] for c in range(8)],
                        [sb_warena[:, 1024 * c + 512 * dh:1024 * c + 512 * dh + 512] for c in range(8)],
                        psx, deps=attnt_done + [f_ph3, pfree])
                    last_wo_mm = fm
                    fadd = O("vector", lambda v, dh2=dh, xt=x1t, ps2=psx, x2=x0r: v.tensor_add(
                        xt[:, 512 * dh2:512 * dh2 + 512], ps2[:, 0:512],
                        x2[:, 512 * dh2:512 * dh2 + 512]),
                        deps=[fm, f_x0r] + ((cs["x1_users"][sl] or []) if dh == 0 else []),
                        inc="dve")
                    proj_free[i] = fadd
                    halves.append(fadd)
                cs["x0rr_slot"][sl] = halves
                f_st = O("sync", lambda s_, xt=x1t, ss=seq, jj=j, g2=g: s_.dma_start(
                    out=x1_sc[ss, BLK * jj + 128 * g2:BLK * jj + 128 * (g2 + 1), :],
                    in_=xt), deps=halves, inc="d_x1_%d" % sl, n=16)
                l2t = sb_ln2[:, D * sl:D * sl + D]
                fln2, _fsq2 = layernorm_tile(x1t, l2t, halves + [cs["ln2_users"][sl]])
                pend[g] = (sl, l2t, fln2)
                cs["x1_users"][sl] = [fln2, f_st]
                blk_fin.extend(halves + [f_st])

                def emit_tr2(g2x):
                    sl2, l2t2, fln2b = pend[g2x]
                    l2slot = sb_ln2t[:, 1024 * sl2:1024 * sl2 + 1024]
                    tr_evs = []
                    last_fm2 = None
                    for half in range(2):
                        i, ps, pfree = next_proj_psum()
                        fm2 = None
                        for k in range(4):
                            c = 4 * half + k
                            fm2 = O("tensor", lambda t, c2=c, k2=k, p=ps, s2=l2t2: t.matmul(
                                p[:, 128 * k2:128 * k2 + 128],
                                s2[:, 128 * c2:128 * c2 + 128], sb_ident,
                                start=True, stop=True),
                                deps=([fln2b, cs["l2t_slot"][sl2], pfree, f_consts]) if k == 0 else [],
                                inc="pe" if k == 3 else None)
                        last_fm2 = fm2
                        dst3 = l2slot.rearrange("p (c n) -> p c n", n=128)[
                            :, 4 * half:4 * half + 4, :]
                        fe = O("vector", lambda v, p=ps, d=dst3: v.tensor_copy(
                            d, p[:, 0:512].rearrange("p (k n) -> p k n", n=128)),
                            deps=[fm2], inc="dve")
                        proj_free[i] = fe
                        tr_evs.append(fe)
                    f_l2st = O("sync", lambda s_, ss=seq, jj=j, g3=g2x, slp=l2slot: s_.dma_start(
                        out=l2_sc[NBLK * ss + jj].rearrange("c p n -> p c n")[:, :, 128 * g3:128 * g3 + 128],
                        in_=slp.rearrange("p (c n) -> p c n", n=128)),
                        deps=tr_evs, inc="d_l2_%d" % sl2, n=16)
                    cs["l2t_slot"][sl2] = f_l2st
                    cs["ln2_users"][sl2] = last_fm2
                    blk_fin.extend(tr_evs + [f_l2st])

                if g >= 1:
                    emit_tr2(g - 1)
                if g == NT - 1:
                    emit_tr2(g)
            cs["arena_users"] = last_wo_mm
            return blk_fin

        pf = {}
        for _ in early_proj_gen(0, pf):
            pass
        late_proj(0, pf)
        for j in range(n_blocks):
            if j + 1 < n_blocks:
                pf_next = {}
                gen = early_proj_gen(j + 1, pf_next)
            else:
                pf_next, gen = None, None
            attn_ev = attention(j, pf, gen)
            if pf_next is not None:
                late_proj(j + 1, pf_next)
            finalize(j, pf_next if pf_next is not None else pf, attn_ev)
            if pf_next is not None:
                pf = pf_next
        cs["attn_seq_done"] = ("pe", bld.cnt.get("pe", 0))
        return


    for s_i in range(n_seqs):
        build_seq(s_i)

    fa16 = Arena(a16_t[:], A16_COLS)
    fa16.off = P_CONST_END
    f_w1r = fa16.alloc(4 * 1024)
    f_w2 = fa16.alloc(32 * D)
    f_ht = fa16.alloc(32 * BLK)
    f_l2t = fa16.alloc(2 * 8 * BLK)
    f_b2v = fa16.alloc(1024)
    fa32 = Arena(a32_t[:], A32_COLS)
    f_x1v = fa32.alloc(2 * D)
    f_x2v = fa32.alloc(2 * D)
    f_b1v = fa32.alloc(32)

    pbar = [("pe", bld.cnt.get("pe", 0)), ("dve", bld.cnt.get("dve", 0)),
            ("act", bld.cnt.get("act", 0))]
    st_totals = [(nm, bld.cnt[nm]) for nm in
                 ("d_x1_0", "d_x1_1", "d_l2_0", "d_l2_1") if nm in bld.cnt]

    fw2 = O("sync", lambda g: g.dma_start(
        out=f_w2.rearrange("p (c n) -> p c n", n=D),
        in_=wsc["w2"].rearrange("(c p) n -> p c n", p=128)),
        deps=pc_totals + pbar, inc="d_f2", n=16)
    fb1 = O("gpsimd", lambda g: g.dma_start(out=f_b1v, in_=w_in["b1l"]),
            deps=pbar, inc="d_f", n=16)
    fb2 = O("gpsimd", lambda g: g.dma_start(
        out=f_b2v[0:1, :], in_=w_in["b2"].unsqueeze(0)),
        deps=[fb1], inc="d_f", n=16)

    fps = {"i": 0, "free": [None, None]}

    def next_ffn_psum():
        i = fps["i"]
        fps["i"] = (i + 1) % 2
        return i, (ps_a, ps_b)[i], fps["free"][i]

    total_blocks = n_seqs * n_blocks
    w1_slot_free = [None, None, None, None]
    l2_free = [None, None]
    x1_slot_free = [None, None]
    x2_slot_free = [None, None]
    psx_free = [cs.get("psx_free")]
    for bi in range(total_blocks):
        s_, j_ = divmod(bi, NBLK)
        es = bi % 2
        l2v = f_l2t[:, 4096 * es:4096 * es + 4096]
        f_l2ld = O("sync", lambda sy, b2_=bi, l2=l2v: sy.dma_start(
            out=l2.rearrange("p (c n) -> p c n", n=512),
            in_=l2_sc[b2_].rearrange("c p n -> p c n")),
            deps=st_totals + [l2_free[es]] + pbar,
            inc="s_fl2_%d" % es, n=16)

        ht_done = []
        for ic in range(32):
            slot = f_w1r[:, 1024 * (ic % 4):1024 * (ic % 4) + 1024]
            f_w1ld = O("sync", lambda sy, sl=slot, ic2=ic: sy.dma_start(
                out=sl.rearrange("p (c n) -> p c n", n=128),
                in_=wsc["w1"].rearrange("(c p) n -> p c n", p=128)[:, :, 128 * ic2:128 * ic2 + 128]),
                deps=pc_totals + [w1_slot_free[ic % 4]] + pbar, inc="s_w1_%d" % (ic % 4), n=16)
            i, ps, pfree = next_ffn_psum()
            fm = proj_chain(
                [slot[:, 128 * c:128 * c + 128] for c in range(8)],
                [l2v[:, 512 * c:512 * c + 512] for c in range(8)],
                ps, deps=[f_l2ld, f_w1ld, pfree, fb1] + pbar)
            fe = O("scalar", lambda sc, p=ps, ic2=ic: sc.activation(
                f_ht[:, 512 * ic2:512 * ic2 + 512], p[:, 0:512], AF.Relu,
                bias=f_b1v[:, ic2:ic2 + 1]),
                deps=[fm], inc="act")
            fps["free"][i] = fe
            w1_slot_free[ic % 4] = fe
            ht_done.append(fe)
        l2_free[es] = ht_done[-1]

        for g in range(NT):
            x1v = f_x1v[:, D * (g % 2):D * (g % 2) + D]
            f_x1ld = O("sync", lambda sy, ss=s_, jj=j_, g2=g, xv=x1v: sy.dma_start(
                out=xv, in_=x1_sc[ss, BLK * jj + 128 * g2:BLK * jj + 128 * (g2 + 1), :]),
                deps=st_totals + [x1_slot_free[g % 2]] + pbar,
                inc="s_fx1_%d" % (g % 2), n=16)
            x2v = f_x2v[:, D * (g % 2):D * (g % 2) + D]
            halves = []
            for dh in range(2):
                i, psx, pfree = next_ffn_psum()
                O("tensor", lambda t, dh2=dh, ps2=psx: t.matmul(
                    ps2[:, 0:512], sb_ones[0:1, :],
                    f_b2v[0:1, 512 * dh2:512 * dh2 + 512],
                    start=True, stop=False),
                    deps=ht_done + [fb2, pfree, x2_slot_free[g % 2], f_cones], inc=None)
                fm = None
                for ic in range(32):
                    fm = O("tensor", lambda t, ic2=ic, g2=g, dh2=dh, ps2=psx: t.matmul(
                        ps2[:, 0:512],
                        f_ht[:, 512 * ic2 + 128 * g2:512 * ic2 + 128 * g2 + 128],
                        f_w2[:, 1024 * ic2 + 512 * dh2:1024 * ic2 + 512 * dh2 + 512],
                        start=False, stop=(ic2 == 31)),
                        deps=[fw2] if ic == 0 else [], inc="pe" if ic == 31 else None)
                fadd = O("vector", lambda v, xv=x2v, dh2=dh, xo=x1v, ps2=psx: v.tensor_add(
                    xv[:, 512 * dh2:512 * dh2 + 512], ps2[:, 0:512],
                    xo[:, 512 * dh2:512 * dh2 + 512]),
                    deps=[fm, f_x1ld], inc="dve")
                fps["free"][i] = fadd
                halves.append(fadd)
                psx_free[0] = fadd
            x1_slot_free[g % 2] = halves[-1]
            fst = O("sync", lambda sy, ss=s_, jj=j_, g2=g, xv=x2v: sy.dma_start(
                out=out[ss, BLK * jj + 128 * g2:BLK * jj + 128 * (g2 + 1), :], in_=xv),
                deps=halves, inc="d_out_%d" % (g % 2), n=16)
            x2_slot_free[g % 2] = fst

    O("gpsimd", lambda g: g.nop(),
      deps=[("d_out_0", bld.cnt.get("d_out_0", 0)),
            ("d_out_1", bld.cnt.get("d_out_1", 0)),
            (("d_dbg", bld.cnt["d_dbg"]) if "d_dbg" in bld.cnt else None)])

    with nc.Block() as block:
        @block.tensor
        def _(e):
            bld.play("tensor", e)

        @block.vector
        def _(e):
            bld.play("vector", e)

        @block.scalar
        def _(e):
            bld.play("scalar", e)

        @block.sync
        def _(e):
            bld.play("sync", e)

        @block.gpsimd
        def _(e):
            bld.play("gpsimd", e)

    st.close()
    return nc


def host_layouts(inputs):
    pair = np.arange(DLAT) ^ 1
    pr32 = np.arange(32) ^ 1
    w = {}
    w["wdqt"] = np.ascontiguousarray(inputs["L0_attn_W_DQ"].T)
    w["wdkvt"] = np.ascontiguousarray(inputs["L0_attn_W_DKV"].T)
    w["wuqt"] = np.ascontiguousarray(inputs["L0_attn_W_UQ"].T)
    wqrt = np.ascontiguousarray(inputs["L0_attn_W_QR"].T)
    w["wqrt"] = wqrt
    w["wqrst"] = np.ascontiguousarray(wqrt[:, pair])
    w["wukt"] = np.ascontiguousarray(inputs["L0_attn_W_UK"].T)
    w["wuvt"] = np.ascontiguousarray(inputs["L0_attn_W_UV"].T)
    wkrt = np.ascontiguousarray(inputs["L0_attn_W_KR"].T)
    w["wkrt"] = np.ascontiguousarray(np.tile(wkrt, (1, 4)))
    w["wkrst"] = np.ascontiguousarray(np.tile(wkrt[:, pr32], (1, 4)))
    w["wot"] = np.ascontiguousarray(inputs["L0_attn_W_O"].T)
    w["w1"] = inputs["L0_ffn_W1"]
    w["w2"] = inputs["L0_ffn_W2"]
    w["b1l"] = np.ascontiguousarray(inputs["L0_ffn_b1"].reshape(32, 128).T)
    w["b2"] = inputs["L0_ffn_b2"]
    return w


def kernel(**inputs):
    x = inputs["L0_attn_x"]
    w = host_layouts(inputs)
    nc = build_program()
    in_maps = []
    for c in range(8):
        m = {"x": np.ascontiguousarray(x[2 * c:2 * c + 2])}
        m.update(w)
        in_maps.append(m)
    res = run_bass_kernel_spmd(nc, in_maps, core_ids=list(range(8)))
    return np.concatenate([r["out"] for r in res.results], axis=0)


if __name__ == "__main__":
    print("building program...")
    nc = build_program()
    n = sum(1 for _ in nc.all_instructions())
    print("instructions:", n)
